# version 19
# baseline (speedup 1.0000x reference)
import contextlib
import dataclasses as _dc
import numpy as np
import ml_dtypes
import concourse.bass as bass
import concourse.mybir as mybir
from concourse.bass_utils import run_bass_kernel_spmd

BF16np = ml_dtypes.bfloat16
F32 = mybir.dt.float32
BF16 = mybir.dt.bfloat16
I32 = mybir.dt.int32
AF = mybir.ActivationFunctionType
ALU = mybir.AluOpType
AX = mybir.AxisListType

S, D, H, HD = 4096, 4096, 32, 128
FRONT = 2048
G = 2
QR = 4
C_SHIFT = -20.0
SQK = 1.0 / float(np.sqrt(128.0))

ENG_OF = {"sp": "sync", "gd": "gpsimd", "gc": "gpsimd",
          "pe": "tensor", "act": "scalar", "dve": "vector"}
DMA_KEYS = ("sp", "gd")


class Buf:
    __slots__ = ("w", "r")

    def __init__(self):
        self.w = {}
        self.r = {}


class Op:
    __slots__ = ("key", "idx", "fn", "deps", "inc", "semval", "is_dma")

    def __init__(self, key, idx, fn, deps, is_dma):
        self.key, self.idx, self.fn, self.deps = key, idx, fn, deps
        self.is_dma = is_dma
        self.inc = False
        self.semval = None


class Planner:
    def __init__(self, nc):
        self.nc = nc
        self.streams = {k: [] for k in ENG_OF}
        self.phys = {"sync": [], "gpsimd": [], "tensor": [], "scalar": [], "vector": []}
        self.sems = {k: nc.alloc_semaphore(f"sem_{k}") for k in ENG_OF}

    def op(self, key, fn, reads=(), writes=(), force=()):
        deps = {}

        def merge(d):
            for k, i in d.items():
                if deps.get(k, -1) < i:
                    deps[k] = i

        for b in reads:
            merge(b.w)
        for b in writes:
            merge(b.w)
            merge(b.r)
        for fk, fi in force:
            if deps.get(fk, -1) < fi:
                deps[fk] = fi
        idx = len(self.streams[key])
        o = Op(key, idx, fn, deps, key in DMA_KEYS)
        self.streams[key].append(o)
        self.phys[ENG_OF[key]].append(o)
        for b in reads:
            if b.r.get(key, -1) < idx:
                b.r[key] = idx
        for b in writes:
            b.w = {key: idx}
            b.r = {}
        return o

    def seed(self, new_bufs, old_bufs):
        m = {}
        for b in old_bufs:
            for d in (b.w, b.r):
                for k, i in d.items():
                    if m.get(k, -1) < i:
                        m[k] = i
        for nb in new_bufs:
            for k, i in m.items():
                if nb.w.get(k, -1) < i:
                    nb.w[k] = i

    def nop_op(self, key, fn, reads=(), force=()):
        deps = {}
        for b in reads:
            for k, i in b.w.items():
                if deps.get(k, -1) < i:
                    deps[k] = i
        for fk, fi in force:
            deps[fk] = max(deps.get(fk, -1), fi)
        idx = len(self.streams[key])
        o = Op(key, idx, fn, deps, False)
        self.streams[key].append(o)
        self.phys[ENG_OF[key]].append(o)
        return o

    @staticmethod
    def _need_edge(dk, d_op, o):
        if dk != o.key:
            return True
        if dk in DMA_KEYS and o.is_dma and d_op.is_dma:
            return False
        if ENG_OF[dk] == "tensor":
            return False
        return True

    def finalize(self):
        for key, ops in self.streams.items():
            for o in ops:
                if o.is_dma:
                    o.inc = True
                for dk, di in o.deps.items():
                    if self._need_edge(dk, self.streams[dk][di], o):
                        self.streams[dk][di].inc = True
        for key, ops in self.streams.items():
            step = 16 if key in DMA_KEYS else 1
            c = 0
            for o in ops:
                if o.inc:
                    c += step
                    o.semval = c

    def emitter(self, phys_name):
        ops = self.phys[phys_name]
        sems = self.sems
        streams = self.streams

        def f(eng):
            hw = {}
            for o in ops:
                for dk in sorted(o.deps):
                    d = streams[dk][o.deps[dk]]
                    if not Planner._need_edge(dk, d, o):
                        continue
                    if d.semval is None:
                        continue
                    if hw.get(dk, 0) < d.semval:
                        eng.wait_ge(sems[dk], d.semval)
                        hw[dk] = d.semval
                inst = o.fn(eng)
                if o.inc:
                    inst.then_inc(sems[o.key], 16 if o.key in DMA_KEYS else 1)
        return f



def make_perm(sub):
    front = np.concatenate(
        [np.arange(256 * j + 128 * sub, 256 * j + 128 * sub + 128) for j in range(16)])
    back = np.concatenate(
        [np.arange(256 * j + 128 * (1 - sub), 256 * j + 128 * (1 - sub) + 128) for j in range(16)])
    return np.concatenate([front, back])


def rotary_tables(pos):
    inv_freq = 1.0 / 10000.0 ** (np.arange(0, HD, 2, dtype=np.float64) / HD)
    fr = pos[None, :].astype(np.float64) * np.concatenate([inv_freq, inv_freq])[:, None]
    cos = np.cos(fr)
    sin = np.sin(fr)
    sinp = np.concatenate([-sin[:64], sin[64:]], axis=0)
    return np.ascontiguousarray(cos.astype(BF16np)), np.ascontiguousarray(sinp.astype(BF16np))


def make_masks(sub):
    kp = np.arange(128)[:, None]
    f = np.arange(512)[None, :]
    c = f // 128
    t = f % 128
    on = np.ones((128, 512), np.float32)
    tiles = [((c > p) | ((c == p) & (t >= kp))).astype(np.float32) for p in range(4)]
    for p in range(4):
        if sub == 0:
            tiles.append(on * (c > p))
        else:
            tiles.append(on * (c >= p))
    return np.ascontiguousarray(np.concatenate(tiles, axis=1).astype(BF16np))



def build_program(debug=False):
    nc = bass.Bass("TRN2", target_bir_lowering=False, debug=False)
    P = Planner(nc)
    op = P.op

    def dram_in(name, shape, dt=F32):
        return nc.dram_tensor(name, shape, dt, kind="ExternalInput").ap()

    x_perm = dram_in("x_perm", [S, D])
    x_perm_bf = dram_in("x_perm_bf", [S, D], BF16)
    wq = dram_in("wq", [32, 128, 4096], BF16)
    wk = dram_in("wk", [32, 128, 4096], BF16)
    wv = dram_in("wv", [8, 128, 32 * 512], BF16)
    wo = dram_in("wo", [16, 128, 32 * 256], BF16)
    WbT = dram_in("WbT", [16, 128, 32 * 256], BF16)
    AT = dram_in("AT", [D, 128], BF16)
    BT = dram_in("BT", [128, D], BF16)
    RT = dram_in("RT", [D, 16], BF16)
    bb = dram_in("bb", [1, D], BF16)
    rb_in = dram_in("rb", [1, 16])
    prior_in = dram_in("prior", [1, 16])
    active_in = dram_in("active", [1, 4], I32)
    iota64_in = dram_in("iota64", [1, 64])
    cosT_in = dram_in("cosT", [HD, S], BF16)
    sinT_in = dram_in("sinT", [HD, S], BF16)
    masks_in = dram_in("masks", [128, 8 * 512], BF16)
    pmat_in = dram_in("pmat", [128, 128], BF16)
    ident_in = dram_in("ident", [128, 128], BF16)

    out_rows = nc.dram_tensor("out_rows", [FRONT, D], F32, kind="ExternalOutput").ap()

    scr_kind = "ExternalOutput" if debug else "Internal"
    k_scr = nc.dram_tensor("k_scr", [H, HD, S], BF16, kind=scr_kind).ap()
    v_scr = nc.dram_tensor("v_scr", [H, 128, 32 * HD], BF16, kind=scr_kind).ap()
    q_scr = nc.dram_tensor("q_scr", [H, HD, FRONT], BF16, kind=scr_kind).ap()
    o_scr = nc.dram_tensor("o_scr", [QR, 128, H * 512], BF16, kind=scr_kind).ap()

    KB = 1024
    ARENA = 196 * KB
    arena = nc.alloc_sbuf_tensor("arena", [128, ARENA], mybir.dt.uint8)
    base = nc.lookup_mloc(arena).addr
    hi = [0]

    def place(name, shape, dt, at):
        per = int(np.prod(shape[1:])) * mybir.dt.size(dt)
        assert at + per <= ARENA, (name, at, per)
        hi[0] = max(hi[0], at + per)
        return nc.alloc_sbuf_tensor_at(name, shape, dt, offset=base + at)

    pa = [0]

    def pplace(name, shape, dt):
        at = pa[0]
        h = place(name, shape, dt, at)
        per = int(np.prod(shape[1:])) * mybir.dt.size(dt)
        pa[0] = at + ((per + 31) // 32) * 32
        return h

    cvec = pplace("cvec", [128, 16], F32)
    ones_col = pplace("ones_col", [1, 128], BF16)
    ones_colf = pplace("ones_colf", [1, 128], F32)
    ones128b = pplace("ones128b", [128, 128], BF16)
    pmat = pplace("pmat_sb", [128, 128], BF16)
    ident = pplace("ident_sb", [128, 128], BF16)
    at_sb = pplace("at_sb", [128, 32 * 128], BF16)
    bt_sb = pplace("bt_sb", [128, D], BF16)
    rt_sb = pplace("rt_sb", [128, 32 * 16], BF16)
    bb_sb = pplace("bb_sb", [1, D], BF16)
    sm = {}
    for n, sh, dt in [("activei", [1, 4], I32), ("activef", [1, 4], F32),
                      ("iota64", [1, 64], F32), ("d64", [1, 64], F32),
                      ("m16", [1, 16], F32), ("prior", [1, 16], F32),
                      ("w16", [1, 16], F32), ("rb", [1, 16], F32),
                      ("rblw", [1, 16], F32), ("rblw_bf", [1, 16], BF16)]:
        sm[n] = pplace("sm_" + n, sh, dt)
    zw4 = [pplace(f"zw4_{i}", [128, 128], BF16) for i in range(4)]
    stv = []
    for i in range(2):
        stv.append({n: pplace(f"st{i}_{n}", [128, 1], F32)
                    for n in ("sx", "sxsq", "mean", "msq", "mean2", "var", "rstd", "nmr")})
    asum = pplace("asum", [128, 1], F32)
    arec = pplace("arec", [128, 1], F32)
    ea_sb = pplace("ea_sb", [128, 16], F32)
    al_sb = pplace("al_sb", [128, 16], BF16)
    zsb = pplace("zsb", [128, 128], BF16)
    U = (pa[0] + 31) // 32 * 32

    hfm = place("hfm", [128, 32 * 2048], BF16, at=U)
    P12 = U + 128 * KB
    x_t = [place(f"x_t{i}", [128, D], BF16, at=P12 + i * 8 * KB) for i in range(2)]
    htok = [place(f"htok{i}", [128, D], BF16, at=P12 + 16 * KB + i * 8 * KB) for i in range(2)]
    wcol = [place(f"wcol{i}", [128, 32 * 128], BF16, at=P12 + i * 8 * KB) for i in range(2)]
    tabc = place("tabc", [128, 2048], BF16, at=P12 + 16 * KB)
    tabs = place("tabs", [128, 2048], BF16, at=P12 + 20 * KB)
    qr_st = [place(f"qr{i}", [128, 512], BF16, at=P12 + 24 * KB + i * KB) for i in range(2)]
    t1_st = [place(f"t1{i}", [128, 512], BF16, at=P12 + 26 * KB + i * KB) for i in range(2)]
    qo_st = [place(f"qo{i}", [128, 512], BF16, at=P12 + 28 * KB + i * KB) for i in range(2)]
    wv_sb = place("wv_sb", [128, 32 * 512], BF16, at=P12)
    vst = [place(f"vst{i}", [128, 512], BF16, at=P12 + 32 * KB + i * KB) for i in range(2)]

    kh_sb = [place(f"kh{i}", [128, S], BF16, at=U + i * 8 * KB) for i in range(2)] + [
        place("kh2", [128, S], BF16, at=U + 64 * KB)]
    vh_sb = [place(f"vh{i}", [128, S], BF16, at=U + 16 * KB + i * 8 * KB) for i in range(2)] + [
        place("vh2", [128, S], BF16, at=U + 72 * KB)]
    qh_sb = [place(f"qh{i}", [128, FRONT], BF16, at=U + 32 * KB + i * 4 * KB) for i in range(2)] + [
        place("qh2", [128, FRONT], BF16, at=U + 80 * KB)]
    pt_sb = [place(f"pt{i}", [128, 1024], BF16, at=U + 40 * KB + i * 2 * KB) for i in range(2)]
    rsum_sb = [place(f"rsum{i}", [128, 512], F32, at=U + 46 * KB + i * 2 * KB) for i in range(2)]
    o_st = [place(f"ost{i}", [128, 512], BF16, at=U + 50 * KB + i * KB) for i in range(2)]
    masks = place("masks_sb", [128, 8 * 512], BF16, at=U + 52 * KB)

    ofm = place("ofm", [128, 32 * 512], BF16, at=U)
    h2fm = place("h2fm", [128, 32 * 512], BF16, at=U)
    wblk = [place(f"wblk{i}", [128, 32 * 256], BF16, at=U + 32 * KB + i * 16 * KB) for i in range(2)]
    a_sb = [place(f"a{i}", [128, D], BF16, at=U + 64 * KB + i * 8 * KB) for i in range(4)]
    ffn_sb = [place(f"ffn{i}", [128, D], BF16, at=U + 96 * KB + i * 8 * KB) for i in range(4)]
    xlnf = place("xlnf", [128, D], F32, at=U + 128 * KB)
    h2tok = place("h2tok", [128, D], BF16, at=U + 144 * KB)
    assert U + 152 * KB <= ARENA

    CV = {"eps": 0, "m05": 1, "m20": 2, "one": 3, "invD": 4, "sqk": 5,
          "tiny": 6, "m1": 7, "e12": 8}
    CVV = {"eps": 1e-5, "m05": -0.5, "m20": C_SHIFT, "one": 1.0,
           "invD": 1.0 / 4096.0, "sqk": SQK, "tiny": 1e-30, "m1": -1.0,
           "e12": 1e-12}

    def cv(name):
        return cvec[:, CV[name]:CV[name] + 1]

    def cv1(name):
        return cvec[0:1, CV[name]:CV[name] + 1]

    def B():
        return Buf()

    b_cvec, b_onescol, b_onesk, b_pmat, b_ident, b_masks = B(), B(), B(), B(), B(), B()
    b_at, b_bt, b_rt, b_bb, b_rblw, b_small = B(), B(), B(), B(), B(), B()
    b_x = [B(), B()]
    b_htok = [[B(), B()], [B(), B()]]
    b_stv = [B(), B()]
    b_hfm = [B() for _ in range(32)]
    b_wcol = [B(), B()]
    b_tab = B()
    b_qr = [B(), B()]
    b_t1 = [B(), B()]
    b_qo = [B(), B()]
    b_wv = B()
    b_vst = [B(), B()]
    b_kscr = [B() for _ in range(H)]
    b_vscr = [B() for _ in range(H)]
    b_qscr = [B() for _ in range(H)]
    b_oscr = [B() for _ in range(H)]
    b_kh = [B(), B(), B()]
    b_vh = [B(), B(), B()]
    b_qh = [B(), B(), B()]
    b_pt = [[B(), B()], [B(), B()]]
    b_rs = [B(), B()]
    b_ost = [B(), B()]
    b_u32 = B()
    b_wblk = [B(), B()]
    b_a = [B() for _ in range(4)]
    b_ffn = [B() for _ in range(4)]
    b_xlnf, b_h2tok = B(), B()
    b_zw4 = [B() for _ in range(4)]
    b_alpha = B()
    b_out = B()

    op("gc", lambda e: e.memset(ones_col[:], 1.0), writes=[b_onescol])
    op("gc", lambda e: e.memset(ones_colf[:], 1.0), writes=[b_onescol])
    op("gc", lambda e: e.memset(ones128b[:], 1.0), writes=[b_onesk])
    for n in CVV:
        j, v = CV[n], CVV[n]
        op("gc", lambda e, j=j, v=v: e.memset(cvec[:, j:j + 1], v), writes=[b_cvec])
    op("sp", lambda e: e.dma_start(out=pmat[:], in_=pmat_in), writes=[b_pmat])
    op("sp", lambda e: e.dma_start(out=ident[:], in_=ident_in), writes=[b_ident])
    op("gd", lambda e: e.dma_start(out=at_sb[:].rearrange("p (c n) -> p c n", c=32),
                                   in_=AT.rearrange("(c p) n -> p c n", p=128)), writes=[b_at])
    op("gd", lambda e: e.dma_start(out=bt_sb[:], in_=BT), writes=[b_bt])
    op("gd", lambda e: e.dma_start(out=rt_sb[:].rearrange("p (c n) -> p c n", c=32),
                                   in_=RT.rearrange("(c p) n -> p c n", p=128)), writes=[b_rt])
    op("gd", lambda e: e.dma_start(out=bb_sb[:], in_=bb), writes=[b_bb])
    op("sp", lambda e: e.dma_start(out=sm["activei"][:], in_=active_in), writes=[b_small])
    op("sp", lambda e: e.dma_start(out=sm["iota64"][:], in_=iota64_in), writes=[b_small])
    op("sp", lambda e: e.dma_start(out=sm["prior"][:], in_=prior_in), writes=[b_small])
    op("sp", lambda e: e.dma_start(out=sm["rb"][:], in_=rb_in), writes=[b_small])

    def bcast_free(ap_, dims):
        return _dc.replace(ap_, ap=[ap_.ap[0]] + dims)

    op("dve", lambda e: e.tensor_copy(sm["activef"][:], sm["activei"][:]),
       reads=[b_small], writes=[b_small])
    a_b = bcast_free(sm["activef"][0:1, 0:4], [[0, 16], [1, 4]])
    i_v = sm["iota64"][0:1, :].rearrange("p (a b) -> p a b", a=16)
    d_v = sm["d64"][0:1, :].rearrange("p (a b) -> p a b", a=16)
    op("dve", lambda e: e.tensor_tensor(d_v, a_b, i_v, ALU.subtract), reads=[b_small], writes=[b_small])
    op("act", lambda e: e.activation(sm["d64"][:], sm["d64"][:], AF.Abs), reads=[b_small], writes=[b_small])
    op("dve", lambda e: e.tensor_scalar(sm["d64"][:], sm["d64"][:], cv1("m1"), cv1("one"), ALU.mult, ALU.add),
       reads=[b_small, b_cvec], writes=[b_small])
    op("dve", lambda e: e.tensor_relu(sm["d64"][:], sm["d64"][:]), reads=[b_small], writes=[b_small])
    m_v = sm["m16"][0:1, :].rearrange("p (a b) -> p a b", a=16)
    op("dve", lambda e: e.tensor_reduce(m_v, d_v, AX.X, ALU.add), reads=[b_small], writes=[b_small])
    op("dve", lambda e: e.tensor_scalar(sm["prior"][:], sm["prior"][:], cv1("one"), cv1("e12"), ALU.mult, ALU.add),
       reads=[b_small, b_cvec], writes=[b_small])
    op("dve", lambda e: e.tensor_tensor(sm["w16"][:], sm["m16"][:], sm["prior"][:], ALU.mult),
       reads=[b_small], writes=[b_small])
    op("act", lambda e: e.activation(sm["w16"][:], sm["w16"][:], AF.Ln, bias=cv1("tiny"), scale=cv1("one")),
       reads=[b_small, b_cvec], writes=[b_small])
    op("dve", lambda e: e.tensor_tensor(sm["rblw"][:], sm["rb"][:], sm["w16"][:], ALU.add),
       reads=[b_small], writes=[b_small])
    op("dve", lambda e: e.tensor_copy(sm["rblw_bf"][:], sm["rblw"][:]), reads=[b_small], writes=[b_rblw])

    hfm_v = hfm[:].rearrange("p (c t) -> p c t", c=32)
    h2fm_v = h2fm[:].rearrange("p (c t) -> p c t", c=32)
    ofm_v = ofm[:].rearrange("p (c t) -> p c t", c=32)

    def ln_stats(x_ap, bufs_in, dump_ap, b_dump, slot):
        v = stv[slot]
        bs = b_stv[slot]
        op("dve", lambda e: e.tensor_reduce(v["sx"][:], x_ap, AX.X, ALU.add),
           reads=list(bufs_in), writes=[bs])
        op("act", lambda e: e.activation(dump_ap, x_ap, AF.Square, accum_out=v["sxsq"][:]),
           reads=list(bufs_in), writes=[b_dump, bs])
        op("dve", lambda e: e.tensor_scalar(v["mean"][:], v["sx"][:], cv("invD"), None, ALU.mult),
           reads=[bs, b_cvec], writes=[bs])
        op("dve", lambda e: e.tensor_scalar(v["msq"][:], v["sxsq"][:], cv("invD"), None, ALU.mult),
           reads=[bs, b_cvec], writes=[bs])
        op("dve", lambda e: e.tensor_tensor(v["mean2"][:], v["mean"][:], v["mean"][:], ALU.mult),
           reads=[bs], writes=[bs])
        op("dve", lambda e: e.tensor_tensor(v["var"][:], v["msq"][:], v["mean2"][:], ALU.subtract),
           reads=[bs], writes=[bs])
        op("act", lambda e: e.activation(v["var"][:], v["var"][:], AF.Ln, bias=cv("eps"), scale=cv("one")),
           reads=[bs, b_cvec], writes=[bs])
        op("act", lambda e: e.activation(v["rstd"][:], v["var"][:], AF.Exp, scale=cv("m05")),
           reads=[bs, b_cvec], writes=[bs])
        op("dve", lambda e: e.tensor_scalar(v["nmr"][:], v["mean"][:], v["rstd"][:], cv("m1"), ALU.mult, ALU.mult),
           reads=[bs, b_cvec], writes=[bs])
        return v, bs

    def ln_small(v, bs):
        op("dve", lambda e: e.tensor_scalar(v["mean"][:], v["sx"][:], cv("invD"), None, ALU.mult),
           reads=[bs, b_cvec], writes=[bs])
        op("dve", lambda e: e.tensor_scalar(v["msq"][:], v["sxsq"][:], cv("invD"), None, ALU.mult),
           reads=[bs, b_cvec], writes=[bs])
        op("dve", lambda e: e.tensor_tensor(v["mean2"][:], v["mean"][:], v["mean"][:], ALU.mult),
           reads=[bs], writes=[bs])
        op("dve", lambda e: e.tensor_tensor(v["var"][:], v["msq"][:], v["mean2"][:], ALU.subtract),
           reads=[bs], writes=[bs])
        op("act", lambda e: e.activation(v["var"][:], v["var"][:], AF.Ln, bias=cv("eps"), scale=cv("one")),
           reads=[bs, b_cvec], writes=[bs])
        op("act", lambda e: e.activation(v["rstd"][:], v["var"][:], AF.Exp, scale=cv("m05")),
           reads=[bs, b_cvec], writes=[bs])
        op("dve", lambda e: e.tensor_scalar(v["nmr"][:], v["mean"][:], v["rstd"][:], cv("m1"), ALU.mult, ALU.mult),
           reads=[bs, b_cvec], writes=[bs])

    pctx = contextlib.ExitStack()
    ps_tr = [pctx.enter_context(nc.psum_tensor(f"ps_tr{i}", [128, 512], BF16)) for i in range(2)]
    ps_qk = [pctx.enter_context(nc.psum_tensor(f"ps_qk{i}", [128, 512], F32)) for i in range(2)]
    ps_rot = [pctx.enter_context(nc.psum_tensor(f"ps_rot{i}", [128, 512], F32)) for i in range(2)]
    ps_v = [pctx.enter_context(nc.psum_tensor(f"ps_v{i}", [128, 512], F32)) for i in range(2)]
    b_pstr = [B(), B()]
    b_psqk = [B(), B()]
    b_psrot = [B(), B()]
    b_psv = [B(), B()]

    p12_qk = [b_wcol[0], b_wcol[1], b_tab, b_qr[0], b_qr[1], b_t1[0], b_t1[1], b_qo[0], b_qo[1]]
    p12_v = [b_wv, b_vst[0], b_vst[1]]
    for g in range(G):
        P.seed(b_x + b_htok[0] + b_htok[1], p12_v + p12_qk)
        for tt in range(16):
            xb_ = tt % 2
            row0 = (g * 16 + tt) * 128
            op("sp", lambda e, row0=row0, xb_=xb_: e.dma_start(out=x_t[xb_][:], in_=x_perm_bf[row0:row0 + 128, :]),
               reads=[b_x[xb_]], writes=[b_x[xb_]])
            v = stv[tt % 2]
            bs = b_stv[tt % 2]
            op("dve", lambda e, xb_=xb_, v=v: e.tensor_reduce(v["sx"][:], x_t[xb_][:], AX.X, ALU.add),
               reads=[b_x[xb_]], writes=[bs])
            op("act", lambda e, xb_=xb_, v=v: e.activation(htok[xb_][:], x_t[xb_][:], AF.Square, accum_out=v["sxsq"][:]),
               reads=[b_x[xb_]], writes=[b_htok[xb_][0], b_htok[xb_][1], bs])
            ln_small(v, bs)
            op("act", lambda e, v=v, xb_=xb_: e.activation(
                htok[xb_][:, 0:2048], x_t[xb_][:, 0:2048], AF.Identity, bias=v["nmr"][:], scale=v["rstd"][:]),
                reads=[b_x[xb_], bs], writes=[b_htok[xb_][0]])
            op("dve", lambda e, v=v, xb_=xb_: e.tensor_scalar(
                htok[xb_][:, 2048:4096], x_t[xb_][:, 2048:4096], v["rstd"][:], v["nmr"][:], ALU.mult, ALU.add),
                reads=[b_x[xb_], bs], writes=[b_htok[xb_][1]])
            for cg in range(8):
                pb = cg % 2
                for i in range(4):
                    c = 4 * cg + i
                    op("pe", lambda e, c=c, pb=pb, i=i, xb_=xb_: e.transpose(
                        ps_tr[pb][:, i * 128:(i + 1) * 128], htok[xb_][:, c * 128:(c + 1) * 128], ident[:]),
                        reads=[b_htok[xb_][c // 16], b_ident], writes=[b_pstr[pb]])
                dst = hfm_v[:, 4 * cg:4 * cg + 4, tt * 128:(tt + 1) * 128]
                src = ps_tr[pb][:].rearrange("p (a b) -> p a b", a=4)
                wl = [b_hfm[4 * cg + i] for i in range(4)]
                op("dve", lambda e, dst=dst, src=src: e.tensor_copy(dst, src),
                   reads=[b_pstr[pb]], writes=wl)

        P.seed(p12_qk, b_x + b_htok[0] + b_htok[1] + p12_v)
        op("sp", lambda e, g=g: e.dma_start(out=tabc[:], in_=cosT_in[:, g * 2048:(g + 1) * 2048]),
           reads=[b_tab], writes=[b_tab])
        op("sp", lambda e, g=g: e.dma_start(out=tabs[:], in_=sinT_in[:, g * 2048:(g + 1) * 2048]),
           reads=[b_tab], writes=[b_tab])

        mats = ([("q", wq, q_scr, 0, b_qscr)] if g == 0 else []) + [("k", wk, k_scr, g * 2048, b_kscr)]
        pend = [None]

        def emit_rot(args):
            qb, ts, dst, bscr_h = args
            op("pe", lambda e, qb=qb: e.matmul(ps_rot[qb][:], pmat[:], qr_st[qb][:], start=True, stop=True),
               reads=[b_qr[qb], b_pmat], writes=[b_psrot[qb]])
            op("dve", lambda e, qb=qb, ts=ts: e.tensor_tensor(t1_st[qb][:], qr_st[qb][:], tabc[:, ts], ALU.mult),
               reads=[b_qr[qb], b_tab], writes=[b_t1[qb]])
            op("dve", lambda e, qb=qb, ts=ts: e.tensor_tensor(qo_st[qb][:], ps_rot[qb][:], tabs[:, ts], ALU.mult),
               reads=[b_psrot[qb], b_tab], writes=[b_qo[qb]])
            op("dve", lambda e, qb=qb: e.tensor_tensor(qo_st[qb][:], qo_st[qb][:], t1_st[qb][:], ALU.add),
               reads=[b_t1[qb], b_qo[qb]], writes=[b_qo[qb]])
            op("sp", lambda e, dst=dst, qb=qb: e.dma_start(out=dst, in_=qo_st[qb][:]),
               reads=[b_qo[qb]], writes=[bscr_h])

        for mname, wsrc, dscr, doff, bscr in mats:
            for hc in range(H):
                wb = hc % 2
                src = wsrc[hc]
                op("gd", lambda e, src=src, wb=wb: e.dma_start(out=wcol[wb][:], in_=src),
                    reads=[b_wcol[wb]], writes=[b_wcol[wb]])
                for tt in range(4):
                    qb = (hc * 4 + tt) % 2
                    for dk in range(32):
                        op("pe", lambda e, wb=wb, dk=dk, tt=tt, qb=qb: e.matmul(
                            ps_qk[qb][:], wcol[wb][:, dk * 128:(dk + 1) * 128],
                            hfm_v[:, dk, tt * 512:(tt + 1) * 512],
                            start=(dk == 0), stop=(dk == 31)),
                            reads=[b_wcol[wb], b_hfm[dk]], writes=[b_psqk[qb]])
                    op("act", lambda e, qb=qb: e.activation(qr_st[qb][:], ps_qk[qb][:], AF.Copy),
                       reads=[b_psqk[qb]], writes=[b_qr[qb]])
                    prev, pend[0] = pend[0], (
                        qb, slice(tt * 512, (tt + 1) * 512),
                        dscr[hc, :, doff + tt * 512: doff + (tt + 1) * 512], bscr[hc])
                    if prev is not None:
                        emit_rot(prev)
        emit_rot(pend[0])
        pend[0] = None

        P.seed(p12_v, p12_qk + b_x + b_htok[0] + b_htok[1])
        for ot in range(8):
            op("gd", lambda e, ot=ot: e.dma_start(out=wv_sb[:], in_=wv[ot]),
               reads=[b_wv], writes=[b_wv])
            for tt in range(16):
                vb = tt % 2
                for dk in range(32):
                    op("pe", lambda e, vb=vb, dk=dk, tt=tt: e.matmul(
                        ps_v[vb][:], hfm_v[:, dk, tt * 128:(tt + 1) * 128],
                        wv_sb[:, dk * 512:(dk + 1) * 512],
                        start=(dk == 0), stop=(dk == 31)),
                        reads=[b_wv, b_hfm[dk]], writes=[b_psv[vb]])
                op("act", lambda e, vb=vb: e.activation(vst[vb][:], ps_v[vb][:], AF.Copy),
                   reads=[b_psv[vb]], writes=[b_vst[vb]])
                tta = g * 16 + tt
                dst = v_scr[4 * ot:4 * ot + 4, :, tta * 128:(tta + 1) * 128].rearrange("h p d -> p h d")
                src2 = vst[vb][:].rearrange("p (h d) -> p h d", h=4)
                wl = [b_vscr[4 * ot + i] for i in range(4)]
                op("sp", lambda e, dst=dst, src2=src2: e.dma_start(out=dst, in_=src2),
                   reads=[b_vst[vb]], writes=wl)

    pctx.close()

    p12_all = p12_qk + p12_v + b_x + b_htok[0] + b_htok[1] + b_hfm
    p3_bufs = b_kh + b_vh + b_qh + b_pt[0] + b_pt[1] + b_rs + b_ost + [b_masks]
    P.seed(p3_bufs, p12_all)
    op("sp", lambda e: e.dma_start(out=masks[:], in_=masks_in),
       reads=[b_masks], writes=[b_masks])

    pctx = contextlib.ExitStack()
    ps_S = [pctx.enter_context(nc.psum_tensor(f"ps_S{i}", [128, 1024], F32)) for i in range(2)]
    ps_o = [pctx.enter_context(nc.psum_tensor(f"ps_o{i}", [128, 512], F32)) for i in range(2)]
    ps_sum = [pctx.enter_context(nc.psum_tensor(f"ps_su{i}", [128, 512], F32)) for i in range(2)]
    b_psS = [B(), B()]
    b_pso = [B(), B()]
    b_pssum = [B(), B()]
    p2_psum = b_pstr + b_psqk + b_psrot + b_psv
    P.seed(b_psS + b_pso + b_pssum, p2_psum)

    for h in range(H):
        hb = h % 3
        op("sp", lambda e, h=h, hb=hb: e.dma_start(out=kh_sb[hb][:], in_=k_scr[h]),
           reads=[b_kscr[h], b_kh[hb]], writes=[b_kh[hb]])
        op("gd", lambda e, h=h, hb=hb: e.dma_start(out=vh_sb[hb][:], in_=v_scr[h]),
            reads=[b_vscr[h], b_vh[hb]], writes=[b_vh[hb]])
        op("sp", lambda e, h=h, hb=hb: e.dma_start(out=qh_sb[hb][:], in_=q_scr[h]),
           reads=[b_qscr[h], b_qh[hb]], writes=[b_qh[hb]])
        for Bq in range(QR):
            ob = Bq % 2
            tiles = ([(j, None) for j in range(4 * Bq)]
                     + [(16 + j, None) for j in range(4 * Bq)]
                     + [(4 * Bq + p, p) for p in range(4)]
                     + [(16 + 4 * Bq + p, 4 + p) for p in range(4)])
            nl = len(tiles)
            qs = slice(Bq * 512, (Bq + 1) * 512)
            npair = nl // 2

            def emit_S_exp(ip, hb=hb, qs=qs, tiles=tiles):
                sb_ = ip % 2
                for half in range(2):
                    kc = tiles[2 * ip + half][0]
                    op("pe", lambda e, sb_=sb_, kc=kc, hb=hb, qs=qs, half=half: e.matmul(
                        ps_S[sb_][:, half * 512:(half + 1) * 512],
                        kh_sb[hb][:, kc * 128:(kc + 1) * 128],
                        qh_sb[hb][:, qs], start=True, stop=True, skip_group_check=True),
                        reads=[b_kh[hb], b_qh[hb]], writes=[b_psS[sb_]])
                op("act", lambda e, sb_=sb_: e.activation(pt_sb[sb_][:], ps_S[sb_][:], AF.Exp, bias=cv("m20"), scale=cv("sqk")),
                   reads=[b_psS[sb_], b_cvec], writes=[b_pt[sb_][0], b_pt[sb_][1]])
                for half in range(2):
                    mi = tiles[2 * ip + half][1]
                    if mi is not None:
                        op("dve", lambda e, sb_=sb_, mi=mi, half=half: e.tensor_tensor(
                            pt_sb[sb_][:, half * 512:(half + 1) * 512],
                            pt_sb[sb_][:, half * 512:(half + 1) * 512],
                            masks[:, mi * 512:(mi + 1) * 512], ALU.mult),
                            reads=[b_pt[sb_][half], b_masks], writes=[b_pt[sb_][half]])

            def emit_PV(ip, hb=hb, ob=ob, nl=nl, tiles=tiles):
                sb_ = ip % 2
                for half in range(2):
                    idx = 2 * ip + half
                    kc = tiles[idx][0]
                    op("pe", lambda e, sb_=sb_, kc=kc, hb=hb, ob=ob, idx=idx, nl=nl, half=half: e.matmul(
                        ps_o[ob][:], vh_sb[hb][:, kc * 128:(kc + 1) * 128],
                        pt_sb[sb_][:, half * 512:(half + 1) * 512],
                        start=(idx == 0), stop=(idx == nl - 1), skip_group_check=True),
                        reads=[b_vh[hb], b_pt[sb_][half]], writes=[b_pso[ob]])
                    op("pe", lambda e, sb_=sb_, ob=ob, idx=idx, nl=nl, half=half: e.matmul(
                        ps_sum[ob][:], ones128b[:],
                        pt_sb[sb_][:, half * 512:(half + 1) * 512],
                        start=(idx == 0), stop=(idx == nl - 1), skip_group_check=True),
                        reads=[b_onesk, b_pt[sb_][half]], writes=[b_pssum[ob]])

            for ip in range(npair + 1):
                if ip < npair:
                    emit_S_exp(ip)
                if ip >= 1:
                    emit_PV(ip - 1)
            op("dve", lambda e, ob=ob: e.reciprocal(rsum_sb[ob][:], ps_sum[ob][:]),
               reads=[b_pssum[ob]], writes=[b_rs[ob]])
            op("dve", lambda e, ob=ob: e.tensor_tensor(o_st[ob][:], ps_o[ob][:], rsum_sb[ob][:], ALU.mult),
               reads=[b_pso[ob], b_rs[ob]], writes=[b_ost[ob]])
            dst = o_scr[Bq, :, h * 512:(h + 1) * 512]
            op("sp", lambda e, dst=dst, ob=ob: e.dma_start(out=dst, in_=o_st[ob][:]),
               reads=[b_ost[ob]], writes=[b_oscr[h]])

    pctx.close()

    p45_bufs = [b_u32, b_wblk[0], b_wblk[1], b_xlnf, b_h2tok] + b_a + b_ffn
    P.seed(p45_bufs, p3_bufs + p12_all)

    pctx = contextlib.ExitStack()
    ps_f = [pctx.enter_context(nc.psum_tensor(f"ps_f{i}", [128, 256], F32)) for i in range(2)]
    ps_z = pctx.enter_context(nc.psum_tensor("ps_z", [128, 128], F32))
    ps_lg = pctx.enter_context(nc.psum_tensor("ps_lg", [128, 16], F32))
    ps_t2 = [pctx.enter_context(nc.psum_tensor(f"ps_t2{i}", [128, 512], BF16)) for i in range(2)]
    ps_zw = pctx.enter_context(nc.psum_tensor("ps_zw", [128, 128], BF16))
    b_psf = [B(), B()]
    b_psz, b_pslg, b_pszw = B(), B(), B()
    b_pst2 = [B(), B()]
    P.seed(b_psf + [b_psz, b_pslg, b_pszw] + b_pst2,
           b_psS + b_pso + b_pssum + p2_psum)

    for r in range(QR):
        op("sp", lambda e, r=r: e.dma_start(out=ofm[:], in_=o_scr[r]),
           reads=b_oscr, writes=[b_u32])
        for cb in range(16):
            wbb = cb % 2
            op("sp" if cb == 0 else "gd",
               lambda e, cb=cb, wbb=wbb: e.dma_start(out=wblk[wbb][:], in_=wo[cb]),
               reads=[b_wblk[wbb]], writes=[b_wblk[wbb]])
            for qt in range(4):
                fb = (cb * 4 + qt) % 2
                for dk in range(32):
                    op("pe", lambda e, fb=fb, wbb=wbb, dk=dk, qt=qt: e.matmul(
                        ps_f[fb][:], ofm_v[:, dk, qt * 128:(qt + 1) * 128],
                        wblk[wbb][:, dk * 256:(dk + 1) * 256],
                        start=(dk == 0), stop=(dk == 31)),
                        reads=[b_u32, b_wblk[wbb]], writes=[b_psf[fb]])
                dst = a_sb[qt][:, cb * 256:(cb + 1) * 256]
                op("act", lambda e, dst=dst, fb=fb: e.activation(dst, ps_f[fb][:], AF.Copy),
                   reads=[b_psf[fb]], writes=[b_a[qt]])
        for qt in range(4):
            row0 = (r * 4 + qt) * 128
            op("sp", lambda e, row0=row0: e.dma_start(out=xlnf[:], in_=x_perm[row0:row0 + 128, :]),
               reads=[b_xlnf], writes=[b_xlnf])
            v, bs = ln_stats(a_sb[qt][:], [b_a[qt]], h2tok[:], b_h2tok, qt % 2)
            op("act", lambda e, v=v: e.activation(xlnf[:], xlnf[:], AF.Identity, bias=v["nmr"][:], scale=cv("one")),
               reads=[b_xlnf, bs, b_cvec], writes=[b_xlnf])
            op("dve", lambda e, qt=qt, v=v: e.scalar_tensor_tensor(a_sb[qt][:], a_sb[qt][:], v["rstd"][:], xlnf[:], ALU.mult, ALU.add),
               reads=[b_a[qt], bs, b_xlnf], writes=[b_a[qt]])
        for qt in range(4):
            v, bs = ln_stats(a_sb[qt][:], [b_a[qt]], h2tok[:], b_h2tok, qt % 2)
            op("act", lambda e, qt=qt, v=v: e.activation(h2tok[:], a_sb[qt][:], AF.Identity, bias=v["nmr"][:], scale=v["rstd"][:]),
               reads=[b_a[qt], bs], writes=[b_h2tok])
            for cg in range(8):
                pb = cg % 2
                for i in range(4):
                    c = 4 * cg + i
                    op("pe", lambda e, c=c, pb=pb, i=i: e.transpose(
                        ps_t2[pb][:, i * 128:(i + 1) * 128], h2tok[:, c * 128:(c + 1) * 128], ident[:]),
                        reads=[b_h2tok, b_ident], writes=[b_pst2[pb]])
                dst = h2fm_v[:, 4 * cg:4 * cg + 4, qt * 128:(qt + 1) * 128]
                src2 = ps_t2[pb][:].rearrange("p (a b) -> p a b", a=4)
                op("dve", lambda e, dst=dst, src2=src2: e.tensor_copy(dst, src2),
                   reads=[b_pst2[pb]], writes=[b_u32])
        for qt in range(4):
            for dk in range(32):
                op("pe", lambda e, dk=dk, qt=qt: e.matmul(
                    ps_z[:], h2fm_v[:, dk, qt * 128:(qt + 1) * 128], at_sb[:, dk * 128:(dk + 1) * 128],
                    start=(dk == 0), stop=(dk == 31)),
                    reads=[b_u32, b_at], writes=[b_psz])
                op("pe", lambda e, dk=dk, qt=qt: e.matmul(
                    ps_lg[:], h2fm_v[:, dk, qt * 128:(qt + 1) * 128], rt_sb[:, dk * 16:(dk + 1) * 16],
                    start=(dk == 0), stop=False),
                    reads=[b_u32, b_rt], writes=[b_pslg])
            op("pe", lambda e: e.matmul(ps_lg[:], ones_col[:], sm["rblw_bf"][0:1, :], start=False, stop=True),
               reads=[b_onescol, b_rblw], writes=[b_pslg])
            op("act", lambda e: e.activation(ea_sb[:], ps_lg[:], AF.Exp, bias=cv("m20"), scale=cv("one"), accum_out=asum[:]),
               reads=[b_pslg, b_cvec], writes=[b_alpha])
            op("dve", lambda e: e.reciprocal(arec[:], asum[:]), reads=[b_alpha], writes=[b_alpha])
            op("dve", lambda e: e.tensor_scalar(al_sb[:], ea_sb[:], arec[:], 2.0, ALU.mult, ALU.mult),
               reads=[b_alpha], writes=[b_alpha])
            al_b = _dc.replace(al_sb[:], ap=[al_sb[:].ap[0], [1, 16], [0, 8]])
            z_b = _dc.replace(ps_z[:], ap=[ps_z[:].ap[0], [8, 16], [1, 8]])
            zv = zsb[:].rearrange("p (a b) -> p a b", a=16)
            op("dve", lambda e, al_b=al_b, z_b=z_b, zv=zv: e.tensor_tensor(zv, z_b, al_b, ALU.mult),
               reads=[b_psz, b_alpha], writes=[b_alpha])
            op("pe", lambda e: e.transpose(ps_zw[:], zsb[:], ident[:]),
               reads=[b_alpha, b_ident], writes=[b_pszw])
            op("dve", lambda e, qt=qt: e.tensor_copy(zw4[qt][:], ps_zw[:]),
               reads=[b_pszw], writes=[b_zw4[qt]])
        for cb in range(16):
            wbb = cb % 2
            op("sp" if cb == 0 else "gd",
               lambda e, cb=cb, wbb=wbb: e.dma_start(out=wblk[wbb][:], in_=WbT[cb]),
               reads=[b_wblk[wbb]], writes=[b_wblk[wbb]])
            for qt in range(4):
                fb = (cb * 4 + qt) % 2
                for dk in range(32):
                    op("pe", lambda e, fb=fb, wbb=wbb, dk=dk, qt=qt: e.matmul(
                        ps_f[fb][:], h2fm_v[:, dk, qt * 128:(qt + 1) * 128],
                        wblk[wbb][:, dk * 256:(dk + 1) * 256],
                        start=(dk == 0), stop=False),
                        reads=[b_u32, b_wblk[wbb]], writes=[b_psf[fb]])
                op("pe", lambda e, fb=fb, cb=cb: e.matmul(
                    ps_f[fb][:], ones_col[:], bb_sb[0:1, cb * 256:(cb + 1) * 256], start=False, stop=False),
                    reads=[b_onescol, b_bb], writes=[b_psf[fb]])
                op("pe", lambda e, fb=fb, cb=cb, qt=qt: e.matmul(
                    ps_f[fb][:], zw4[qt][:], bt_sb[:, cb * 256:(cb + 1) * 256], start=False, stop=True),
                    reads=[b_zw4[qt], b_bt], writes=[b_psf[fb]])
                dst = ffn_sb[qt][:, cb * 256:(cb + 1) * 256]
                op("act", lambda e, dst=dst, fb=fb: e.activation(dst, ps_f[fb][:], AF.Copy),
                   reads=[b_psf[fb]], writes=[b_ffn[qt]])
        for qt in range(4):
            row0 = (r * 4 + qt) * 128
            v, bs = ln_stats(ffn_sb[qt][:], [b_ffn[qt]], h2tok[:], b_h2tok, qt % 2)
            for half in range(2):
                hs = slice(half * 2048, (half + 1) * 2048)
                lnf = xlnf[:, 0:2048]
                outp = xlnf[:, 2048:4096]
                op("act", lambda e, qt=qt, hs=hs, v=v, lnf=lnf: e.activation(
                    lnf, ffn_sb[qt][:, hs], AF.Identity, bias=v["nmr"][:], scale=v["rstd"][:]),
                    reads=[b_ffn[qt], bs], writes=[b_xlnf])
                op("dve", lambda e, qt=qt, hs=hs, lnf=lnf, outp=outp: e.tensor_tensor(
                    outp, lnf, a_sb[qt][:, hs], ALU.add),
                    reads=[b_xlnf, b_a[qt]], writes=[b_xlnf])
                dst = out_rows[row0:row0 + 128, half * 2048:(half + 1) * 2048]
                op("sp", lambda e, dst=dst, outp=outp: e.dma_start(out=dst, in_=outp),
                   reads=[b_xlnf], writes=[b_out])

    pctx.close()

    P.nop_op("sp", lambda e: e.nop(), reads=[b_out])

    P.finalize()
    with nc.Block() as block:
        block.sync(P.emitter("sync"))
        block.gpsimd(P.emitter("gpsimd"))
        block.tensor(P.emitter("tensor"))
        block.scalar(P.emitter("scalar"))
        block.vector(P.emitter("vector"))
    return nc



_CACHE = {}


def _shared_inputs(inputs):
    gi = {k: np.asarray(v) for k, v in inputs.items()}
    sh = {}
    def _wlay(w, nb, wid):
        w4 = np.asarray(w).astype(BF16np).reshape(32, 128, nb, wid)
        return np.ascontiguousarray(w4.transpose(2, 1, 0, 3).reshape(nb, 128, 32 * wid))
    sh["wq"] = _wlay(gi["L0_attn_wq"], 32, 128)
    sh["wk"] = _wlay(gi["L0_attn_wk"], 32, 128)
    sh["wv"] = _wlay(gi["L0_attn_wv"], 8, 512)
    sh["wo"] = _wlay(gi["L0_attn_wo"], 16, 256)
    sh["WbT"] = _wlay(gi["L0_ffn_W_base"].T, 16, 256)
    sh["AT"] = np.ascontiguousarray(
        gi["L0_ffn_lora_A"].astype(np.float32).transpose(2, 0, 1).reshape(D, 128).astype(BF16np))
    sh["BT"] = np.ascontiguousarray(
        gi["L0_ffn_lora_B"].astype(np.float32).transpose(0, 2, 1).reshape(128, D).astype(BF16np))
    sh["RT"] = np.ascontiguousarray(gi["L0_ffn_router_W"].astype(np.float32).T.astype(BF16np))
    sh["bb"] = np.ascontiguousarray(gi["L0_ffn_b_base"].astype(np.float32).reshape(1, D).astype(BF16np))
    sh["rb"] = np.ascontiguousarray(gi["L0_ffn_router_b"].astype(np.float32).reshape(1, 16))
    sh["prior"] = np.ascontiguousarray(gi["L0_ffn_domain_prior"].astype(np.float32).reshape(1, 16))
    sh["active"] = np.ascontiguousarray(gi["L0_ffn_active"].astype(np.int32).reshape(1, 4))
    sh["iota64"] = np.ascontiguousarray(np.repeat(np.arange(16, dtype=np.float32), 4).reshape(1, 64))
    pm = np.zeros((128, 128), np.float32)
    for i in range(128):
        pm[(i + 64) % 128, i] = 1.0
    sh["pmat"] = pm.astype(BF16np)
    sh["ident"] = np.eye(128, dtype=np.float32).astype(BF16np)
    return gi, sh


def kernel(**inputs):
    gi, sh = _shared_inputs(inputs)
    x = np.asarray(gi["L0_attn_hidden_states"], np.float32)

    if "nc" not in _CACHE:
        _CACHE["nc"] = build_program()
    nc = _CACHE["nc"]

    in_maps = []
    perms = []
    for c in range(8):
        batch, sub = c // 2, c % 2
        perm = make_perm(sub)
        perms.append(perm)
        cosT, sinT = rotary_tables(perm)
        m = dict(sh)
        xp = np.ascontiguousarray(x[batch][perm])
        m["x_perm"] = xp
        m["x_perm_bf"] = np.ascontiguousarray(xp.astype(BF16np))
        m["cosT"] = cosT
        m["sinT"] = sinT
        m["masks"] = make_masks(sub)
        in_maps.append(m)

    res = run_bass_kernel_spmd(nc, in_maps, core_ids=list(range(8)))

    out = np.empty((4, S, D), dtype=np.float32)
    for c in range(8):
        batch = c // 2
        out[batch][perms[c][:FRONT]] = res.results[c]["out_rows"]
    return out


# version 20
# speedup vs baseline: 1.1883x; 1.1883x over previous
import contextlib
import dataclasses as _dc
import numpy as np
import ml_dtypes
import concourse.bass as bass
import concourse.mybir as mybir
from concourse.bass_utils import run_bass_kernel_spmd

BF16np = ml_dtypes.bfloat16
F32 = mybir.dt.float32
BF16 = mybir.dt.bfloat16
I32 = mybir.dt.int32
AF = mybir.ActivationFunctionType
ALU = mybir.AluOpType
AX = mybir.AxisListType

S, D, H, HD = 4096, 4096, 32, 128
FRONT = 2048
G = 2
QR = 4
C_SHIFT = -20.0
SQK = 1.0 / float(np.sqrt(128.0))

ENG_OF = {"sp": "sync", "gd": "gpsimd", "gc": "gpsimd",
          "pe": "tensor", "act": "scalar", "dve": "vector"}
DMA_KEYS = ("sp", "gd")


class Buf:
    __slots__ = ("w", "r")

    def __init__(self):
        self.w = {}
        self.r = {}


class Op:
    __slots__ = ("key", "idx", "fn", "deps", "inc", "semval", "is_dma")

    def __init__(self, key, idx, fn, deps, is_dma):
        self.key, self.idx, self.fn, self.deps = key, idx, fn, deps
        self.is_dma = is_dma
        self.inc = False
        self.semval = None


class Planner:
    def __init__(self, nc):
        self.nc = nc
        self.streams = {k: [] for k in ENG_OF}
        self.phys = {"sync": [], "gpsimd": [], "tensor": [], "scalar": [], "vector": []}
        self.sems = {k: nc.alloc_semaphore(f"sem_{k}") for k in ENG_OF}

    def op(self, key, fn, reads=(), writes=(), force=()):
        deps = {}

        def merge(d):
            for k, i in d.items():
                if deps.get(k, -1) < i:
                    deps[k] = i

        for b in reads:
            merge(b.w)
        for b in writes:
            merge(b.w)
            merge(b.r)
        for fk, fi in force:
            if deps.get(fk, -1) < fi:
                deps[fk] = fi
        idx = len(self.streams[key])
        o = Op(key, idx, fn, deps, key in DMA_KEYS)
        self.streams[key].append(o)
        self.phys[ENG_OF[key]].append(o)
        for b in reads:
            if b.r.get(key, -1) < idx:
                b.r[key] = idx
        for b in writes:
            b.w = {key: idx}
            b.r = {}
        return o

    def seed(self, new_bufs, old_bufs):
        m = {}
        for b in old_bufs:
            for d in (b.w, b.r):
                for k, i in d.items():
                    if m.get(k, -1) < i:
                        m[k] = i
        for nb in new_bufs:
            for k, i in m.items():
                if nb.w.get(k, -1) < i:
                    nb.w[k] = i

    def nop_op(self, key, fn, reads=(), force=()):
        deps = {}
        for b in reads:
            for k, i in b.w.items():
                if deps.get(k, -1) < i:
                    deps[k] = i
        for fk, fi in force:
            deps[fk] = max(deps.get(fk, -1), fi)
        idx = len(self.streams[key])
        o = Op(key, idx, fn, deps, False)
        self.streams[key].append(o)
        self.phys[ENG_OF[key]].append(o)
        return o

    @staticmethod
    def _need_edge(dk, d_op, o):
        if dk != o.key:
            return True
        if dk in DMA_KEYS and o.is_dma and d_op.is_dma:
            return False
        if ENG_OF[dk] == "tensor":
            return False
        return True

    def finalize(self):
        for key, ops in self.streams.items():
            for o in ops:
                if o.is_dma:
                    o.inc = True
                for dk, di in o.deps.items():
                    if self._need_edge(dk, self.streams[dk][di], o):
                        self.streams[dk][di].inc = True
        for key, ops in self.streams.items():
            step = 16 if key in DMA_KEYS else 1
            c = 0
            for o in ops:
                if o.inc:
                    c += step
                    o.semval = c

    def emitter(self, phys_name):
        ops = self.phys[phys_name]
        sems = self.sems
        streams = self.streams

        def f(eng):
            hw = {}
            for o in ops:
                for dk in sorted(o.deps):
                    d = streams[dk][o.deps[dk]]
                    if not Planner._need_edge(dk, d, o):
                        continue
                    if d.semval is None:
                        continue
                    if hw.get(dk, 0) < d.semval:
                        eng.wait_ge(sems[dk], d.semval)
                        hw[dk] = d.semval
                inst = o.fn(eng)
                if o.inc:
                    inst.then_inc(sems[o.key], 16 if o.key in DMA_KEYS else 1)
        return f



def make_perm(sub):
    front = np.concatenate(
        [np.arange(256 * j + 128 * sub, 256 * j + 128 * sub + 128) for j in range(16)])
    back = np.concatenate(
        [np.arange(256 * j + 128 * (1 - sub), 256 * j + 128 * (1 - sub) + 128) for j in range(16)])
    return np.concatenate([front, back])


def rotary_tables(pos):
    inv_freq = 1.0 / 10000.0 ** (np.arange(0, HD, 2, dtype=np.float64) / HD)
    fr = pos[None, :].astype(np.float64) * np.concatenate([inv_freq, inv_freq])[:, None]
    cos = np.cos(fr)
    sin = np.sin(fr)
    sinp = np.concatenate([-sin[:64], sin[64:]], axis=0)
    return np.ascontiguousarray(cos.astype(BF16np)), np.ascontiguousarray(sinp.astype(BF16np))


def make_masks(sub):
    kp = np.arange(128)[:, None]
    f = np.arange(512)[None, :]
    c = f // 128
    t = f % 128
    on = np.ones((128, 512), np.float32)
    tiles = [((c > p) | ((c == p) & (t >= kp))).astype(np.float32) for p in range(4)]
    for p in range(4):
        if sub == 0:
            tiles.append(on * (c > p))
        else:
            tiles.append(on * (c >= p))
    return np.ascontiguousarray(np.concatenate(tiles, axis=1).astype(BF16np))



def build_program(debug=False):
    nc = bass.Bass("TRN2", target_bir_lowering=False, debug=False)
    P = Planner(nc)
    op = P.op

    def dram_in(name, shape, dt=F32):
        return nc.dram_tensor(name, shape, dt, kind="ExternalInput").ap()

    x_perm = dram_in("x_perm", [S, D])
    x_perm_bf = dram_in("x_perm_bf", [S, D], BF16)
    wq = dram_in("wq", [32, 128, 4096], BF16)
    wk = dram_in("wk", [32, 128, 4096], BF16)
    wv = dram_in("wv", [8, 128, 32 * 512], BF16)
    wo = dram_in("wo", [16, 128, 32 * 256], BF16)
    WbT = dram_in("WbT", [16, 128, 32 * 256], BF16)
    AT = dram_in("AT", [D, 128], BF16)
    BT = dram_in("BT", [128, D], BF16)
    RT = dram_in("RT", [D, 16], BF16)
    bb = dram_in("bb", [1, D], BF16)
    rb_in = dram_in("rb", [1, 16])
    prior_in = dram_in("prior", [1, 16])
    active_in = dram_in("active", [1, 4], I32)
    iota64_in = dram_in("iota64", [1, 64])
    cosT_in = dram_in("cosT", [HD, S], BF16)
    sinT_in = dram_in("sinT", [HD, S], BF16)
    masks_in = dram_in("masks", [128, 8 * 512], BF16)
    pmat_in = dram_in("pmat", [128, 128], BF16)
    ident_in = dram_in("ident", [128, 128], BF16)

    out_rows = nc.dram_tensor("out_rows", [FRONT, D], F32, kind="ExternalOutput").ap()

    scr_kind = "ExternalOutput" if debug else "Internal"
    k_scr = nc.dram_tensor("k_scr", [H, HD, S], BF16, kind=scr_kind).ap()
    v_scr = nc.dram_tensor("v_scr", [H, 128, 32 * HD], BF16, kind=scr_kind).ap()
    q_scr = nc.dram_tensor("q_scr", [H, HD, FRONT], BF16, kind=scr_kind).ap()
    o_scr = nc.dram_tensor("o_scr", [QR, 128, H * 512], BF16, kind=scr_kind).ap()

    KB = 1024
    ARENA = 196 * KB
    arena = nc.alloc_sbuf_tensor("arena", [128, ARENA], mybir.dt.uint8)
    base = nc.lookup_mloc(arena).addr
    hi = [0]

    def place(name, shape, dt, at):
        per = int(np.prod(shape[1:])) * mybir.dt.size(dt)
        assert at + per <= ARENA, (name, at, per)
        hi[0] = max(hi[0], at + per)
        return nc.alloc_sbuf_tensor_at(name, shape, dt, offset=base + at)

    pa = [0]

    def pplace(name, shape, dt):
        at = pa[0]
        h = place(name, shape, dt, at)
        per = int(np.prod(shape[1:])) * mybir.dt.size(dt)
        pa[0] = at + ((per + 31) // 32) * 32
        return h

    cvec = pplace("cvec", [128, 16], F32)
    ones_col = pplace("ones_col", [1, 128], BF16)
    ones_colf = pplace("ones_colf", [1, 128], F32)
    ones128b = pplace("ones128b", [128, 128], BF16)
    pmat = pplace("pmat_sb", [128, 128], BF16)
    ident = pplace("ident_sb", [128, 128], BF16)
    at_sb = pplace("at_sb", [128, 32 * 128], BF16)
    bt_sb = pplace("bt_sb", [128, D], BF16)
    rt_sb = pplace("rt_sb", [128, 32 * 16], BF16)
    bb_sb = pplace("bb_sb", [1, D], BF16)
    sm = {}
    for n, sh, dt in [("activei", [1, 4], I32), ("activef", [1, 4], F32),
                      ("iota64", [1, 64], F32), ("d64", [1, 64], F32),
                      ("m16", [1, 16], F32), ("prior", [1, 16], F32),
                      ("w16", [1, 16], F32), ("rb", [1, 16], F32),
                      ("rblw", [1, 16], F32), ("rblw_bf", [1, 16], BF16)]:
        sm[n] = pplace("sm_" + n, sh, dt)
    zw4 = [pplace(f"zw4_{i}", [128, 128], BF16) for i in range(4)]
    stv = []
    for i in range(2):
        stv.append({n: pplace(f"st{i}_{n}", [128, 1], F32)
                    for n in ("sx", "sxsq", "mean", "msq", "mean2", "var", "rstd", "nmr")})
    asum = pplace("asum", [128, 1], F32)
    arec = pplace("arec", [128, 1], F32)
    ea_sb = pplace("ea_sb", [128, 16], F32)
    al_sb = pplace("al_sb", [128, 16], BF16)
    zsb = pplace("zsb", [128, 128], BF16)
    U = (pa[0] + 31) // 32 * 32

    hfm = place("hfm", [128, 32 * 2048], BF16, at=U)
    P12 = U + 128 * KB
    x_t = [place(f"x_t{i}", [128, D], BF16, at=P12 + i * 8 * KB) for i in range(2)]
    htok = [place(f"htok{i}", [128, D], BF16, at=P12 + 16 * KB + i * 8 * KB) for i in range(2)]
    wcol = [place(f"wcol{i}", [128, 32 * 128], BF16, at=P12 + i * 8 * KB) for i in range(2)]
    tabc = place("tabc", [128, 2048], BF16, at=P12 + 16 * KB)
    tabs = place("tabs", [128, 2048], BF16, at=P12 + 20 * KB)
    qr_st = [place(f"qr{i}", [128, 512], BF16, at=P12 + 24 * KB + i * KB) for i in range(2)]
    t1_st = [place(f"t1{i}", [128, 512], BF16, at=P12 + 26 * KB + i * KB) for i in range(2)]
    qo_st = [place(f"qo{i}", [128, 512], BF16, at=P12 + 28 * KB + i * KB) for i in range(2)]
    wv_sb = place("wv_sb", [128, 32 * 512], BF16, at=P12)
    vst = [place(f"vst{i}", [128, 512], BF16, at=P12 + 32 * KB + i * KB) for i in range(2)]

    kh_sb = [place(f"kh{i}", [128, S], BF16, at=U + i * 8 * KB) for i in range(2)] + [
        place("kh2", [128, S], BF16, at=U + 64 * KB)]
    vh_sb = [place(f"vh{i}", [128, S], BF16, at=U + 16 * KB + i * 8 * KB) for i in range(2)] + [
        place("vh2", [128, S], BF16, at=U + 72 * KB)]
    qh_sb = [place(f"qh{i}", [128, FRONT], BF16, at=U + 32 * KB + i * 4 * KB) for i in range(2)] + [
        place("qh2", [128, FRONT], BF16, at=U + 80 * KB)]
    pt_sb = [place(f"pt{i}", [128, 1024], BF16, at=U + 40 * KB + i * 2 * KB) for i in range(2)]
    rsum_sb = [place(f"rsum{i}", [128, 512], F32, at=U + 46 * KB + i * 2 * KB) for i in range(2)]
    o_st = [place(f"ost{i}", [128, 512], BF16, at=U + 50 * KB + i * KB) for i in range(2)]
    masks = place("masks_sb", [128, 8 * 512], BF16, at=U + 52 * KB)

    ofm = place("ofm", [128, 32 * 512], BF16, at=U)
    h2fm = place("h2fm", [128, 32 * 512], BF16, at=U)
    wblk = [place(f"wblk{i}", [128, 32 * 256], BF16, at=U + 32 * KB + i * 16 * KB) for i in range(2)]
    a_sb = [place(f"a{i}", [128, D], BF16, at=U + 64 * KB + i * 8 * KB) for i in range(4)]
    ffn_sb = [place(f"ffn{i}", [128, D], BF16, at=U + 96 * KB + i * 8 * KB) for i in range(4)]
    xlnf = place("xlnf", [128, D], F32, at=U + 128 * KB)
    h2tok = place("h2tok", [128, D], BF16, at=U + 144 * KB)
    assert U + 152 * KB <= ARENA

    CV = {"eps": 0, "m05": 1, "m20": 2, "one": 3, "invD": 4, "sqk": 5,
          "tiny": 6, "m1": 7, "e12": 8}
    CVV = {"eps": 1e-5, "m05": -0.5, "m20": C_SHIFT, "one": 1.0,
           "invD": 1.0 / 4096.0, "sqk": SQK, "tiny": 1e-30, "m1": -1.0,
           "e12": 1e-12}

    def cv(name):
        return cvec[:, CV[name]:CV[name] + 1]

    def cv1(name):
        return cvec[0:1, CV[name]:CV[name] + 1]

    def B():
        return Buf()

    b_cvec, b_onescol, b_onesk, b_pmat, b_ident, b_masks = B(), B(), B(), B(), B(), B()
    b_at, b_bt, b_rt, b_bb, b_rblw, b_small = B(), B(), B(), B(), B(), B()
    b_x = [B(), B()]
    b_htok = [[B(), B()], [B(), B()]]
    b_stv = [B(), B()]
    b_hfm = [B() for _ in range(32)]
    b_wcol = [B(), B()]
    b_tab = B()
    b_qr = [B(), B()]
    b_t1 = [B(), B()]
    b_qo = [B(), B()]
    b_wv = B()
    b_vst = [B(), B()]
    b_kscr = [B() for _ in range(H)]
    b_vscr = [B() for _ in range(H)]
    b_qscr = [B() for _ in range(H)]
    b_oscr = [B() for _ in range(H)]
    b_kh = [B(), B(), B()]
    b_vh = [B(), B(), B()]
    b_qh = [B(), B(), B()]
    b_pt = [[B(), B()], [B(), B()]]
    b_rs = [B(), B()]
    b_ost = [B(), B()]
    b_u32 = B()
    b_wblk = [B(), B()]
    b_a = [B() for _ in range(4)]
    b_ffn = [B() for _ in range(4)]
    b_xlnf, b_h2tok = B(), B()
    b_zw4 = [B() for _ in range(4)]
    b_alpha = B()
    b_out = B()

    op("gc", lambda e: e.memset(ones_col[:], 1.0), writes=[b_onescol])
    op("gc", lambda e: e.memset(ones_colf[:], 1.0), writes=[b_onescol])
    op("gc", lambda e: e.memset(ones128b[:], 1.0), writes=[b_onesk])
    for n in CVV:
        j, v = CV[n], CVV[n]
        op("gc", lambda e, j=j, v=v: e.memset(cvec[:, j:j + 1], v), writes=[b_cvec])
    op("sp", lambda e: e.dma_start(out=pmat[:], in_=pmat_in), writes=[b_pmat])
    op("sp", lambda e: e.dma_start(out=ident[:], in_=ident_in), writes=[b_ident])
    op("gd", lambda e: e.dma_start(out=at_sb[:].rearrange("p (c n) -> p c n", c=32),
                                   in_=AT.rearrange("(c p) n -> p c n", p=128)), writes=[b_at])
    op("gd", lambda e: e.dma_start(out=bt_sb[:], in_=BT), writes=[b_bt])
    op("gd", lambda e: e.dma_start(out=rt_sb[:].rearrange("p (c n) -> p c n", c=32),
                                   in_=RT.rearrange("(c p) n -> p c n", p=128)), writes=[b_rt])
    op("gd", lambda e: e.dma_start(out=bb_sb[:], in_=bb), writes=[b_bb])
    op("sp", lambda e: e.dma_start(out=sm["activei"][:], in_=active_in), writes=[b_small])
    op("sp", lambda e: e.dma_start(out=sm["iota64"][:], in_=iota64_in), writes=[b_small])
    op("sp", lambda e: e.dma_start(out=sm["prior"][:], in_=prior_in), writes=[b_small])
    op("sp", lambda e: e.dma_start(out=sm["rb"][:], in_=rb_in), writes=[b_small])

    def bcast_free(ap_, dims):
        return _dc.replace(ap_, ap=[ap_.ap[0]] + dims)

    op("dve", lambda e: e.tensor_copy(sm["activef"][:], sm["activei"][:]),
       reads=[b_small], writes=[b_small])
    a_b = bcast_free(sm["activef"][0:1, 0:4], [[0, 16], [1, 4]])
    i_v = sm["iota64"][0:1, :].rearrange("p (a b) -> p a b", a=16)
    d_v = sm["d64"][0:1, :].rearrange("p (a b) -> p a b", a=16)
    op("dve", lambda e: e.tensor_tensor(d_v, a_b, i_v, ALU.subtract), reads=[b_small], writes=[b_small])
    op("act", lambda e: e.activation(sm["d64"][:], sm["d64"][:], AF.Abs), reads=[b_small], writes=[b_small])
    op("dve", lambda e: e.tensor_scalar(sm["d64"][:], sm["d64"][:], cv1("m1"), cv1("one"), ALU.mult, ALU.add),
       reads=[b_small, b_cvec], writes=[b_small])
    op("dve", lambda e: e.tensor_relu(sm["d64"][:], sm["d64"][:]), reads=[b_small], writes=[b_small])
    m_v = sm["m16"][0:1, :].rearrange("p (a b) -> p a b", a=16)
    op("dve", lambda e: e.tensor_reduce(m_v, d_v, AX.X, ALU.add), reads=[b_small], writes=[b_small])
    op("dve", lambda e: e.tensor_scalar(sm["prior"][:], sm["prior"][:], cv1("one"), cv1("e12"), ALU.mult, ALU.add),
       reads=[b_small, b_cvec], writes=[b_small])
    op("dve", lambda e: e.tensor_tensor(sm["w16"][:], sm["m16"][:], sm["prior"][:], ALU.mult),
       reads=[b_small], writes=[b_small])
    op("act", lambda e: e.activation(sm["w16"][:], sm["w16"][:], AF.Ln, bias=cv1("tiny"), scale=cv1("one")),
       reads=[b_small, b_cvec], writes=[b_small])
    op("dve", lambda e: e.tensor_tensor(sm["rblw"][:], sm["rb"][:], sm["w16"][:], ALU.add),
       reads=[b_small], writes=[b_small])
    op("dve", lambda e: e.tensor_copy(sm["rblw_bf"][:], sm["rblw"][:]), reads=[b_small], writes=[b_rblw])

    hfm_v = hfm[:].rearrange("p (c t) -> p c t", c=32)
    h2fm_v = h2fm[:].rearrange("p (c t) -> p c t", c=32)
    ofm_v = ofm[:].rearrange("p (c t) -> p c t", c=32)

    def ln_stats(x_ap, bufs_in, dump_ap, b_dump, slot):
        v = stv[slot]
        bs = b_stv[slot]
        op("dve", lambda e: e.tensor_reduce(v["sx"][:], x_ap, AX.X, ALU.add),
           reads=list(bufs_in), writes=[bs])
        op("act", lambda e: e.activation(dump_ap, x_ap, AF.Square, accum_out=v["sxsq"][:]),
           reads=list(bufs_in), writes=[b_dump, bs])
        op("dve", lambda e: e.tensor_scalar(v["mean"][:], v["sx"][:], cv("invD"), None, ALU.mult),
           reads=[bs, b_cvec], writes=[bs])
        op("dve", lambda e: e.tensor_scalar(v["msq"][:], v["sxsq"][:], cv("invD"), None, ALU.mult),
           reads=[bs, b_cvec], writes=[bs])
        op("dve", lambda e: e.tensor_tensor(v["mean2"][:], v["mean"][:], v["mean"][:], ALU.mult),
           reads=[bs], writes=[bs])
        op("dve", lambda e: e.tensor_tensor(v["var"][:], v["msq"][:], v["mean2"][:], ALU.subtract),
           reads=[bs], writes=[bs])
        op("act", lambda e: e.activation(v["var"][:], v["var"][:], AF.Ln, bias=cv("eps"), scale=cv("one")),
           reads=[bs, b_cvec], writes=[bs])
        op("act", lambda e: e.activation(v["rstd"][:], v["var"][:], AF.Exp, scale=cv("m05")),
           reads=[bs, b_cvec], writes=[bs])
        op("dve", lambda e: e.tensor_scalar(v["nmr"][:], v["mean"][:], v["rstd"][:], cv("m1"), ALU.mult, ALU.mult),
           reads=[bs, b_cvec], writes=[bs])
        return v, bs

    def ln_small(v, bs):
        op("dve", lambda e: e.tensor_scalar(v["mean"][:], v["sx"][:], cv("invD"), None, ALU.mult),
           reads=[bs, b_cvec], writes=[bs])
        op("dve", lambda e: e.tensor_scalar(v["msq"][:], v["sxsq"][:], cv("invD"), None, ALU.mult),
           reads=[bs, b_cvec], writes=[bs])
        op("dve", lambda e: e.tensor_tensor(v["mean2"][:], v["mean"][:], v["mean"][:], ALU.mult),
           reads=[bs], writes=[bs])
        op("dve", lambda e: e.tensor_tensor(v["var"][:], v["msq"][:], v["mean2"][:], ALU.subtract),
           reads=[bs], writes=[bs])
        op("act", lambda e: e.activation(v["var"][:], v["var"][:], AF.Ln, bias=cv("eps"), scale=cv("one")),
           reads=[bs, b_cvec], writes=[bs])
        op("act", lambda e: e.activation(v["rstd"][:], v["var"][:], AF.Exp, scale=cv("m05")),
           reads=[bs, b_cvec], writes=[bs])
        op("dve", lambda e: e.tensor_scalar(v["nmr"][:], v["mean"][:], v["rstd"][:], cv("m1"), ALU.mult, ALU.mult),
           reads=[bs, b_cvec], writes=[bs])

    pctx = contextlib.ExitStack()
    ps_tr = [pctx.enter_context(nc.psum_tensor(f"ps_tr{i}", [128, 512], BF16)) for i in range(2)]
    ps_qk = [pctx.enter_context(nc.psum_tensor(f"ps_qk{i}", [128, 512], F32)) for i in range(2)]
    ps_rot = [pctx.enter_context(nc.psum_tensor(f"ps_rot{i}", [128, 512], F32)) for i in range(2)]
    ps_v = [pctx.enter_context(nc.psum_tensor(f"ps_v{i}", [128, 512], F32)) for i in range(2)]
    b_pstr = [B(), B()]
    b_psqk = [B(), B()]
    b_psrot = [B(), B()]
    b_psv = [B(), B()]

    p12_qk = [b_wcol[0], b_wcol[1], b_tab, b_qr[0], b_qr[1], b_t1[0], b_t1[1], b_qo[0], b_qo[1]]
    p12_v = [b_wv, b_vst[0], b_vst[1]]
    for g in range(G):
        P.seed(b_x + b_htok[0] + b_htok[1], p12_v + p12_qk)
        for tt in range(16):
            xb_ = tt % 2
            row0 = (g * 16 + tt) * 128
            op("sp", lambda e, row0=row0, xb_=xb_: e.dma_start(out=x_t[xb_][:], in_=x_perm_bf[row0:row0 + 128, :]),
               reads=[b_x[xb_]], writes=[b_x[xb_]])
            v = stv[tt % 2]
            bs = b_stv[tt % 2]
            op("dve", lambda e, xb_=xb_, v=v: e.tensor_reduce(v["sx"][:], x_t[xb_][:], AX.X, ALU.add),
               reads=[b_x[xb_]], writes=[bs])
            op("act", lambda e, xb_=xb_, v=v: e.activation(htok[xb_][:], x_t[xb_][:], AF.Square, accum_out=v["sxsq"][:]),
               reads=[b_x[xb_]], writes=[b_htok[xb_][0], b_htok[xb_][1], bs])
            ln_small(v, bs)
            op("act", lambda e, v=v, xb_=xb_: e.activation(
                htok[xb_][:, 0:2048], x_t[xb_][:, 0:2048], AF.Identity, bias=v["nmr"][:], scale=v["rstd"][:]),
                reads=[b_x[xb_], bs], writes=[b_htok[xb_][0]])
            op("dve", lambda e, v=v, xb_=xb_: e.tensor_scalar(
                htok[xb_][:, 2048:4096], x_t[xb_][:, 2048:4096], v["rstd"][:], v["nmr"][:], ALU.mult, ALU.add),
                reads=[b_x[xb_], bs], writes=[b_htok[xb_][1]])
            for cg in range(8):
                pb = cg % 2
                for i in range(4):
                    c = 4 * cg + i
                    op("pe", lambda e, c=c, pb=pb, i=i, xb_=xb_: e.transpose(
                        ps_tr[pb][:, i * 128:(i + 1) * 128], htok[xb_][:, c * 128:(c + 1) * 128], ident[:]),
                        reads=[b_htok[xb_][c // 16], b_ident], writes=[b_pstr[pb]])
                dst = hfm_v[:, 4 * cg:4 * cg + 4, tt * 128:(tt + 1) * 128]
                src = ps_tr[pb][:].rearrange("p (a b) -> p a b", a=4)
                wl = [b_hfm[4 * cg + i] for i in range(4)]
                op("dve", lambda e, dst=dst, src=src: e.tensor_copy(dst, src),
                   reads=[b_pstr[pb]], writes=wl)

        P.seed(p12_qk, b_x + b_htok[0] + b_htok[1] + p12_v)
        op("sp", lambda e, g=g: e.dma_start(out=tabc[:], in_=cosT_in[:, g * 2048:(g + 1) * 2048]),
           reads=[b_tab], writes=[b_tab])
        op("sp", lambda e, g=g: e.dma_start(out=tabs[:], in_=sinT_in[:, g * 2048:(g + 1) * 2048]),
           reads=[b_tab], writes=[b_tab])

        mats = ([("q", wq, q_scr, 0, b_qscr)] if g == 0 else []) + [("k", wk, k_scr, g * 2048, b_kscr)]
        pend = [None]

        def emit_rot(args):
            qb, ts, dst, bscr_h = args
            op("pe", lambda e, qb=qb: e.matmul(ps_rot[qb][:], pmat[:], qr_st[qb][:], start=True, stop=True),
               reads=[b_qr[qb], b_pmat], writes=[b_psrot[qb]])
            op("dve", lambda e, qb=qb, ts=ts: e.tensor_tensor(t1_st[qb][:], qr_st[qb][:], tabc[:, ts], ALU.mult),
               reads=[b_qr[qb], b_tab], writes=[b_t1[qb]])
            op("dve", lambda e, qb=qb, ts=ts: e.tensor_tensor(qo_st[qb][:], ps_rot[qb][:], tabs[:, ts], ALU.mult),
               reads=[b_psrot[qb], b_tab], writes=[b_qo[qb]])
            op("dve", lambda e, qb=qb: e.tensor_tensor(qo_st[qb][:], qo_st[qb][:], t1_st[qb][:], ALU.add),
               reads=[b_t1[qb], b_qo[qb]], writes=[b_qo[qb]])
            op("sp", lambda e, dst=dst, qb=qb: e.dma_start(out=dst, in_=qo_st[qb][:]),
               reads=[b_qo[qb]], writes=[bscr_h])

        for mname, wsrc, dscr, doff, bscr in mats:
            for hc in range(H):
                wb = hc % 2
                src = wsrc[hc]
                op("gd", lambda e, src=src, wb=wb: e.dma_start(out=wcol[wb][:], in_=src),
                    reads=[b_wcol[wb]], writes=[b_wcol[wb]])
                for tt in range(4):
                    qb = (hc * 4 + tt) % 2
                    for dk in range(32):
                        op("pe", lambda e, wb=wb, dk=dk, tt=tt, qb=qb: e.matmul(
                            ps_qk[qb][:], wcol[wb][:, dk * 128:(dk + 1) * 128],
                            hfm_v[:, dk, tt * 512:(tt + 1) * 512],
                            start=(dk == 0), stop=(dk == 31)),
                            reads=[b_wcol[wb], b_hfm[dk]], writes=[b_psqk[qb]])
                    op("act", lambda e, qb=qb: e.activation(qr_st[qb][:], ps_qk[qb][:], AF.Copy),
                       reads=[b_psqk[qb]], writes=[b_qr[qb]])
                    prev, pend[0] = pend[0], (
                        qb, slice(tt * 512, (tt + 1) * 512),
                        dscr[hc, :, doff + tt * 512: doff + (tt + 1) * 512], bscr[hc])
                    if prev is not None:
                        emit_rot(prev)
        emit_rot(pend[0])
        pend[0] = None

        P.seed(p12_v, p12_qk + b_x + b_htok[0] + b_htok[1])
        for ot in range(8):
            op("gd", lambda e, ot=ot: e.dma_start(out=wv_sb[:], in_=wv[ot]),
               reads=[b_wv], writes=[b_wv])
            for tt in range(16):
                vb = tt % 2
                for dk in range(32):
                    op("pe", lambda e, vb=vb, dk=dk, tt=tt: e.matmul(
                        ps_v[vb][:], hfm_v[:, dk, tt * 128:(tt + 1) * 128],
                        wv_sb[:, dk * 512:(dk + 1) * 512],
                        start=(dk == 0), stop=(dk == 31)),
                        reads=[b_wv, b_hfm[dk]], writes=[b_psv[vb]])
                op("act", lambda e, vb=vb: e.activation(vst[vb][:], ps_v[vb][:], AF.Copy),
                   reads=[b_psv[vb]], writes=[b_vst[vb]])
                tta = g * 16 + tt
                dst = v_scr[4 * ot:4 * ot + 4, :, tta * 128:(tta + 1) * 128].rearrange("h p d -> p h d")
                src2 = vst[vb][:].rearrange("p (h d) -> p h d", h=4)
                wl = [b_vscr[4 * ot + i] for i in range(4)]
                op("sp", lambda e, dst=dst, src2=src2: e.dma_start(out=dst, in_=src2),
                   reads=[b_vst[vb]], writes=wl)

    pctx.close()

    p12_all = p12_qk + p12_v + b_x + b_htok[0] + b_htok[1] + b_hfm
    p3_bufs = b_kh + b_vh + b_qh + b_pt[0] + b_pt[1] + b_rs + b_ost + [b_masks]
    P.seed(p3_bufs, p12_all)
    op("sp", lambda e: e.dma_start(out=masks[:], in_=masks_in),
       reads=[b_masks], writes=[b_masks])

    pctx = contextlib.ExitStack()
    ps_S = [pctx.enter_context(nc.psum_tensor(f"ps_S{i}", [128, 1024], F32)) for i in range(2)]
    ps_o = [pctx.enter_context(nc.psum_tensor(f"ps_o{i}", [128, 512], F32)) for i in range(2)]
    ps_sum = [pctx.enter_context(nc.psum_tensor(f"ps_su{i}", [128, 512], F32)) for i in range(2)]
    b_psS = [B(), B()]
    b_pso = [B(), B()]
    b_pssum = [B(), B()]
    p2_psum = b_pstr + b_psqk + b_psrot + b_psv
    P.seed(b_psS + b_pso + b_pssum, p2_psum)

    for h in range(H):
        hb = h % 3
        op("sp", lambda e, h=h, hb=hb: e.dma_start(out=kh_sb[hb][:], in_=k_scr[h]),
           reads=[b_kscr[h], b_kh[hb]], writes=[b_kh[hb]])
        op("gd", lambda e, h=h, hb=hb: e.dma_start(out=vh_sb[hb][:], in_=v_scr[h]),
            reads=[b_vscr[h], b_vh[hb]], writes=[b_vh[hb]])
        op("sp", lambda e, h=h, hb=hb: e.dma_start(out=qh_sb[hb][:], in_=q_scr[h]),
           reads=[b_qscr[h], b_qh[hb]], writes=[b_qh[hb]])
        for Bq in range(QR):
            ob = Bq % 2
            tiles = ([(j, None) for j in range(4 * Bq)]
                     + [(16 + j, None) for j in range(4 * Bq)]
                     + [(4 * Bq + p, p) for p in range(4)]
                     + [(16 + 4 * Bq + p, 4 + p) for p in range(4)])
            nl = len(tiles)
            qs = slice(Bq * 512, (Bq + 1) * 512)
            npair = nl // 2

            def emit_S_exp(ip, hb=hb, qs=qs, tiles=tiles):
                sb_ = ip % 2
                for half in range(2):
                    kc = tiles[2 * ip + half][0]
                    op("pe", lambda e, sb_=sb_, kc=kc, hb=hb, qs=qs, half=half: e.matmul(
                        ps_S[sb_][:, half * 512:(half + 1) * 512],
                        kh_sb[hb][:, kc * 128:(kc + 1) * 128],
                        qh_sb[hb][:, qs], start=True, stop=True, skip_group_check=True),
                        reads=[b_kh[hb], b_qh[hb]], writes=[b_psS[sb_]])
                op("act", lambda e, sb_=sb_: e.activation(pt_sb[sb_][:], ps_S[sb_][:], AF.Exp, bias=cv("m20"), scale=cv("sqk")),
                   reads=[b_psS[sb_], b_cvec], writes=[b_pt[sb_][0], b_pt[sb_][1]])
                for half in range(2):
                    mi = tiles[2 * ip + half][1]
                    if mi is not None:
                        op("dve", lambda e, sb_=sb_, mi=mi, half=half: e.tensor_tensor(
                            pt_sb[sb_][:, half * 512:(half + 1) * 512],
                            pt_sb[sb_][:, half * 512:(half + 1) * 512],
                            masks[:, mi * 512:(mi + 1) * 512], ALU.mult),
                            reads=[b_pt[sb_][half], b_masks], writes=[b_pt[sb_][half]])

            def emit_PV(ip, hb=hb, ob=ob, nl=nl, tiles=tiles):
                sb_ = ip % 2
                for half in range(2):
                    idx = 2 * ip + half
                    kc = tiles[idx][0]
                    op("pe", lambda e, sb_=sb_, kc=kc, hb=hb, ob=ob, idx=idx, nl=nl, half=half: e.matmul(
                        ps_o[ob][:], vh_sb[hb][:, kc * 128:(kc + 1) * 128],
                        pt_sb[sb_][:, half * 512:(half + 1) * 512],
                        start=(idx == 0), stop=(idx == nl - 1), skip_group_check=True),
                        reads=[b_vh[hb], b_pt[sb_][half]], writes=[b_pso[ob]])
                    op("pe", lambda e, sb_=sb_, ob=ob, idx=idx, nl=nl, half=half: e.matmul(
                        ps_sum[ob][:], ones128b[:],
                        pt_sb[sb_][:, half * 512:(half + 1) * 512],
                        start=(idx == 0), stop=(idx == nl - 1), skip_group_check=True),
                        reads=[b_onesk, b_pt[sb_][half]], writes=[b_pssum[ob]])

            for ip in range(npair + 1):
                if ip < npair:
                    emit_S_exp(ip)
                if ip >= 1:
                    emit_PV(ip - 1)
            op("dve", lambda e, ob=ob: e.reciprocal(rsum_sb[ob][:], ps_sum[ob][:]),
               reads=[b_pssum[ob]], writes=[b_rs[ob]])
            op("dve", lambda e, ob=ob: e.tensor_tensor(o_st[ob][:], ps_o[ob][:], rsum_sb[ob][:], ALU.mult),
               reads=[b_pso[ob], b_rs[ob]], writes=[b_ost[ob]])
            dst = o_scr[Bq, :, h * 512:(h + 1) * 512]
            op("sp", lambda e, dst=dst, ob=ob: e.dma_start(out=dst, in_=o_st[ob][:]),
               reads=[b_ost[ob]], writes=[b_oscr[h]])

    pctx.close()

    p45_bufs = [b_u32, b_wblk[0], b_wblk[1], b_xlnf, b_h2tok] + b_a + b_ffn
    P.seed(p45_bufs, p3_bufs + p12_all)

    pctx = contextlib.ExitStack()
    ps_f = [pctx.enter_context(nc.psum_tensor(f"ps_f{i}", [128, 256], F32)) for i in range(2)]
    ps_z = pctx.enter_context(nc.psum_tensor("ps_z", [128, 128], F32))
    ps_lg = pctx.enter_context(nc.psum_tensor("ps_lg", [128, 16], F32))
    ps_t2 = [pctx.enter_context(nc.psum_tensor(f"ps_t2{i}", [128, 512], BF16)) for i in range(2)]
    ps_zw = pctx.enter_context(nc.psum_tensor("ps_zw", [128, 128], BF16))
    b_psf = [B(), B()]
    b_psz, b_pslg, b_pszw = B(), B(), B()
    b_pst2 = [B(), B()]
    P.seed(b_psf + [b_psz, b_pslg, b_pszw] + b_pst2,
           b_psS + b_pso + b_pssum + p2_psum)

    op("sp", lambda e: e.dma_start(out=ofm[:], in_=o_scr[0]),
       reads=b_oscr, writes=[b_u32])
    for r in range(QR):
        for cb in range(16):
            wbb = cb % 2
            op("gd", lambda e, cb=cb, wbb=wbb: e.dma_start(out=wblk[wbb][:], in_=wo[cb]),
                reads=[b_wblk[wbb]], writes=[b_wblk[wbb]])
            for qt in range(4):
                fb = (cb * 4 + qt) % 2
                for dk in range(32):
                    op("pe", lambda e, fb=fb, wbb=wbb, dk=dk, qt=qt: e.matmul(
                        ps_f[fb][:], ofm_v[:, dk, qt * 128:(qt + 1) * 128],
                        wblk[wbb][:, dk * 256:(dk + 1) * 256],
                        start=(dk == 0), stop=(dk == 31)),
                        reads=[b_u32, b_wblk[wbb]], writes=[b_psf[fb]])
                dst = a_sb[qt][:, cb * 256:(cb + 1) * 256]
                op("act", lambda e, dst=dst, fb=fb: e.activation(dst, ps_f[fb][:], AF.Copy),
                   reads=[b_psf[fb]], writes=[b_a[qt]])
        for qt in range(4):
            row0 = (r * 4 + qt) * 128
            op("sp", lambda e, row0=row0: e.dma_start(out=xlnf[:], in_=x_perm[row0:row0 + 128, :]),
               reads=[b_xlnf], writes=[b_xlnf])
            v, bs = ln_stats(a_sb[qt][:], [b_a[qt]], h2tok[:], b_h2tok, qt % 2)
            op("act", lambda e, v=v: e.activation(xlnf[:], xlnf[:], AF.Identity, bias=v["nmr"][:], scale=cv("one")),
               reads=[b_xlnf, bs, b_cvec], writes=[b_xlnf])
            op("dve", lambda e, qt=qt, v=v: e.scalar_tensor_tensor(a_sb[qt][:], a_sb[qt][:], v["rstd"][:], xlnf[:], ALU.mult, ALU.add),
               reads=[b_a[qt], bs, b_xlnf], writes=[b_a[qt]])
        for qt in range(4):
            v, bs = ln_stats(a_sb[qt][:], [b_a[qt]], h2tok[:], b_h2tok, qt % 2)
            op("act", lambda e, qt=qt, v=v: e.activation(h2tok[:], a_sb[qt][:], AF.Identity, bias=v["nmr"][:], scale=v["rstd"][:]),
               reads=[b_a[qt], bs], writes=[b_h2tok])
            for cg in range(8):
                pb = cg % 2
                for i in range(4):
                    c = 4 * cg + i
                    op("pe", lambda e, c=c, pb=pb, i=i: e.transpose(
                        ps_t2[pb][:, i * 128:(i + 1) * 128], h2tok[:, c * 128:(c + 1) * 128], ident[:]),
                        reads=[b_h2tok, b_ident], writes=[b_pst2[pb]])
                dst = h2fm_v[:, 4 * cg:4 * cg + 4, qt * 128:(qt + 1) * 128]
                src2 = ps_t2[pb][:].rearrange("p (a b) -> p a b", a=4)
                op("dve", lambda e, dst=dst, src2=src2: e.tensor_copy(dst, src2),
                   reads=[b_pst2[pb]], writes=[b_u32])
        for qt in range(4):
            for dk in range(32):
                op("pe", lambda e, dk=dk, qt=qt: e.matmul(
                    ps_z[:], h2fm_v[:, dk, qt * 128:(qt + 1) * 128], at_sb[:, dk * 128:(dk + 1) * 128],
                    start=(dk == 0), stop=(dk == 31)),
                    reads=[b_u32, b_at], writes=[b_psz])
                op("pe", lambda e, dk=dk, qt=qt: e.matmul(
                    ps_lg[:], h2fm_v[:, dk, qt * 128:(qt + 1) * 128], rt_sb[:, dk * 16:(dk + 1) * 16],
                    start=(dk == 0), stop=False),
                    reads=[b_u32, b_rt], writes=[b_pslg])
            op("pe", lambda e: e.matmul(ps_lg[:], ones_col[:], sm["rblw_bf"][0:1, :], start=False, stop=True),
               reads=[b_onescol, b_rblw], writes=[b_pslg])
            op("act", lambda e: e.activation(ea_sb[:], ps_lg[:], AF.Exp, bias=cv("m20"), scale=cv("one"), accum_out=asum[:]),
               reads=[b_pslg, b_cvec], writes=[b_alpha])
            op("dve", lambda e: e.reciprocal(arec[:], asum[:]), reads=[b_alpha], writes=[b_alpha])
            op("dve", lambda e: e.tensor_scalar(al_sb[:], ea_sb[:], arec[:], 2.0, ALU.mult, ALU.mult),
               reads=[b_alpha], writes=[b_alpha])
            al_b = _dc.replace(al_sb[:], ap=[al_sb[:].ap[0], [1, 16], [0, 8]])
            z_b = _dc.replace(ps_z[:], ap=[ps_z[:].ap[0], [8, 16], [1, 8]])
            zv = zsb[:].rearrange("p (a b) -> p a b", a=16)
            op("dve", lambda e, al_b=al_b, z_b=z_b, zv=zv: e.tensor_tensor(zv, z_b, al_b, ALU.mult),
               reads=[b_psz, b_alpha], writes=[b_alpha])
            op("pe", lambda e: e.transpose(ps_zw[:], zsb[:], ident[:]),
               reads=[b_alpha, b_ident], writes=[b_pszw])
            op("dve", lambda e, qt=qt: e.tensor_copy(zw4[qt][:], ps_zw[:]),
               reads=[b_pszw], writes=[b_zw4[qt]])
        for cb in range(16):
            wbb = cb % 2
            op("gd", lambda e, cb=cb, wbb=wbb: e.dma_start(out=wblk[wbb][:], in_=WbT[cb]),
                reads=[b_wblk[wbb]], writes=[b_wblk[wbb]])
            for qt in range(4):
                fb = (cb * 4 + qt) % 2
                for dk in range(32):
                    op("pe", lambda e, fb=fb, wbb=wbb, dk=dk, qt=qt: e.matmul(
                        ps_f[fb][:], h2fm_v[:, dk, qt * 128:(qt + 1) * 128],
                        wblk[wbb][:, dk * 256:(dk + 1) * 256],
                        start=(dk == 0), stop=False),
                        reads=[b_u32, b_wblk[wbb]], writes=[b_psf[fb]])
                op("pe", lambda e, fb=fb, cb=cb: e.matmul(
                    ps_f[fb][:], ones_col[:], bb_sb[0:1, cb * 256:(cb + 1) * 256], start=False, stop=False),
                    reads=[b_onescol, b_bb], writes=[b_psf[fb]])
                op("pe", lambda e, fb=fb, cb=cb, qt=qt: e.matmul(
                    ps_f[fb][:], zw4[qt][:], bt_sb[:, cb * 256:(cb + 1) * 256], start=False, stop=True),
                    reads=[b_zw4[qt], b_bt], writes=[b_psf[fb]])
                dst = ffn_sb[qt][:, cb * 256:(cb + 1) * 256]
                op("act", lambda e, dst=dst, fb=fb: e.activation(dst, ps_f[fb][:], AF.Copy),
                   reads=[b_psf[fb]], writes=[b_ffn[qt]])
        if r + 1 < QR:
            op("sp", lambda e, r=r: e.dma_start(out=ofm[:], in_=o_scr[r + 1]),
               reads=b_oscr, writes=[b_u32])
        for qt in range(4):
            row0 = (r * 4 + qt) * 128
            v, bs = ln_stats(ffn_sb[qt][:], [b_ffn[qt]], h2tok[:], b_h2tok, qt % 2)
            for half in range(2):
                hs = slice(half * 2048, (half + 1) * 2048)
                lnf = xlnf[:, 0:2048]
                outp = xlnf[:, 2048:4096]
                op("act", lambda e, qt=qt, hs=hs, v=v, lnf=lnf: e.activation(
                    lnf, ffn_sb[qt][:, hs], AF.Identity, bias=v["nmr"][:], scale=v["rstd"][:]),
                    reads=[b_ffn[qt], bs], writes=[b_xlnf])
                op("dve", lambda e, qt=qt, hs=hs, lnf=lnf, outp=outp: e.tensor_tensor(
                    outp, lnf, a_sb[qt][:, hs], ALU.add),
                    reads=[b_xlnf, b_a[qt]], writes=[b_xlnf])
                dst = out_rows[row0:row0 + 128, half * 2048:(half + 1) * 2048]
                op("sp", lambda e, dst=dst, outp=outp: e.dma_start(out=dst, in_=outp),
                   reads=[b_xlnf], writes=[b_out])

    pctx.close()

    P.nop_op("sp", lambda e: e.nop(), reads=[b_out])

    P.finalize()
    with nc.Block() as block:
        block.sync(P.emitter("sync"))
        block.gpsimd(P.emitter("gpsimd"))
        block.tensor(P.emitter("tensor"))
        block.scalar(P.emitter("scalar"))
        block.vector(P.emitter("vector"))
    return nc



_CACHE = {}


def _shared_inputs(inputs):
    gi = {k: np.asarray(v) for k, v in inputs.items()}
    sh = {}
    def _wlay(w, nb, wid):
        w4 = np.asarray(w).astype(BF16np).reshape(32, 128, nb, wid)
        return np.ascontiguousarray(w4.transpose(2, 1, 0, 3).reshape(nb, 128, 32 * wid))
    sh["wq"] = _wlay(gi["L0_attn_wq"], 32, 128)
    sh["wk"] = _wlay(gi["L0_attn_wk"], 32, 128)
    sh["wv"] = _wlay(gi["L0_attn_wv"], 8, 512)
    sh["wo"] = _wlay(gi["L0_attn_wo"], 16, 256)
    sh["WbT"] = _wlay(gi["L0_ffn_W_base"].T, 16, 256)
    sh["AT"] = np.ascontiguousarray(
        gi["L0_ffn_lora_A"].astype(np.float32).transpose(2, 0, 1).reshape(D, 128).astype(BF16np))
    sh["BT"] = np.ascontiguousarray(
        gi["L0_ffn_lora_B"].astype(np.float32).transpose(0, 2, 1).reshape(128, D).astype(BF16np))
    sh["RT"] = np.ascontiguousarray(gi["L0_ffn_router_W"].astype(np.float32).T.astype(BF16np))
    sh["bb"] = np.ascontiguousarray(gi["L0_ffn_b_base"].astype(np.float32).reshape(1, D).astype(BF16np))
    sh["rb"] = np.ascontiguousarray(gi["L0_ffn_router_b"].astype(np.float32).reshape(1, 16))
    sh["prior"] = np.ascontiguousarray(gi["L0_ffn_domain_prior"].astype(np.float32).reshape(1, 16))
    sh["active"] = np.ascontiguousarray(gi["L0_ffn_active"].astype(np.int32).reshape(1, 4))
    sh["iota64"] = np.ascontiguousarray(np.repeat(np.arange(16, dtype=np.float32), 4).reshape(1, 64))
    pm = np.zeros((128, 128), np.float32)
    for i in range(128):
        pm[(i + 64) % 128, i] = 1.0
    sh["pmat"] = pm.astype(BF16np)
    sh["ident"] = np.eye(128, dtype=np.float32).astype(BF16np)
    return gi, sh


def kernel(**inputs):
    gi, sh = _shared_inputs(inputs)
    x = np.asarray(gi["L0_attn_hidden_states"], np.float32)

    if "nc" not in _CACHE:
        _CACHE["nc"] = build_program()
    nc = _CACHE["nc"]

    in_maps = []
    perms = []
    for c in range(8):
        batch, sub = c // 2, c % 2
        perm = make_perm(sub)
        perms.append(perm)
        cosT, sinT = rotary_tables(perm)
        m = dict(sh)
        xp = np.ascontiguousarray(x[batch][perm])
        m["x_perm"] = xp
        m["x_perm_bf"] = np.ascontiguousarray(xp.astype(BF16np))
        m["cosT"] = cosT
        m["sinT"] = sinT
        m["masks"] = make_masks(sub)
        in_maps.append(m)

    res = run_bass_kernel_spmd(nc, in_maps, core_ids=list(range(8)))

    out = np.empty((4, S, D), dtype=np.float32)
    for c in range(8):
        batch = c // 2
        out[batch][perms[c][:FRONT]] = res.results[c]["out_rows"]
    return out
